# version 09 (final)
# baseline (speedup 1.0000x reference)
import numpy as np
from contextlib import ExitStack, contextmanager
import concourse.bass as bass
import concourse.mybir as mybir
from concourse.alu_op_type import AluOpType as OP
from concourse.bass_utils import run_bass_kernel_spmd

F32 = mybir.dt.float32
AF = mybir.ActivationFunctionType
AX = mybir.AxisListType

P = 128
NH = 16
D = 128
DM = 2048
TLOC = 2048
NT = 16
NCORE = 8
INTER = 8192
MINTER = 1024
SCALE = float(D) ** -0.5

ENGS = ("tensor", "vector", "scalar", "sync", "gpsimd")


@contextmanager
def scope(*cms):
    with ExitStack() as st:
        yield tuple(st.enter_context(c) for c in cms)


class EM:
    def __init__(self, block, sems):
        self.block = block
        self.sem = sems
        self.cnt = dict.fromkeys(ENGS, 0)
        self.seen = {e: dict.fromkeys(ENGS, 0) for e in ENGS}

    def run(self, eng, fn, self_wait=False):
        if eng in ("vector", "scalar"):
            self_wait = True
        def body(px):
            for f in ENGS:
                if f == eng and not self_wait:
                    continue
                if self.cnt[f] > self.seen[eng][f]:
                    px.wait_ge(self.sem[f], self.cnt[f])
                    self.seen[eng][f] = self.cnt[f]
            insts = fn(px)
            if insts is None:
                insts = []
            if not isinstance(insts, (list, tuple)):
                insts = [insts]
            step = 16 if eng in ("sync", "gpsimd") else 1
            for ins in insts:
                if ins is None:
                    continue
                ins.then_inc(self.sem[eng], step)
                self.cnt[eng] += step
        getattr(self.block, eng)(body)

    def wait_all(self, eng):
        def body(px):
            for f in ENGS:
                if self.cnt[f] > self.seen[eng][f]:
                    px.wait_ge(self.sem[f], self.cnt[f])
                    self.seen[eng][f] = self.cnt[f]
        getattr(self.block, eng)(body)


def build():
    nc = bass.Bass("TRN2", target_bir_lowering=False)
    dt = nc.dram_tensor
    x_in = dt("x_in", [TLOC + P, DM], F32, kind="ExternalInput").ap()
    Wq = dt("Wq", [DM, DM], F32, kind="ExternalInput").ap()
    Wk = dt("Wk", [DM, DM], F32, kind="ExternalInput").ap()
    Wv = dt("Wv", [DM, DM], F32, kind="ExternalInput").ap()
    Wb = dt("Wb", [DM, NH], F32, kind="ExternalInput").ap()
    Wfa = dt("Wfa", [DM, D], F32, kind="ExternalInput").ap()
    Wfb = dt("Wfb", [D, DM], F32, kind="ExternalInput").ap()
    Wga = dt("Wga", [DM, D], F32, kind="ExternalInput").ap()
    Wgb = dt("Wgb", [D, DM], F32, kind="ExternalInput").ap()
    Wo = dt("Wo", [DM, DM], F32, kind="ExternalInput").ap()
    cw_all = dt("cw_all", [P, 3 * NH, 4], F32, kind="ExternalInput").ap()
    db_t = dt("db_t", [P, NH], F32, kind="ExternalInput").ap()
    Arep = dt("Arep", [P, NH], F32, kind="ExternalInput").ap()
    rms_in = dt("rms_in", [P, D], F32, kind="ExternalInput").ap()
    cvec = dt("cvec", [1, DM], F32, kind="ExternalInput").ap()
    wmod = dt("wmod", [DM, 6 * DM], F32, kind="ExternalInput").ap()
    shgT = dt("shgT", [DM, INTER], F32, kind="ExternalInput").ap()
    shuT = dt("shuT", [DM, INTER], F32, kind="ExternalInput").ap()
    shdT = dt("shdT", [INTER, DM], F32, kind="ExternalInput").ap()
    rgT = dt("rgT", [DM, MINTER], F32, kind="ExternalInput").ap()
    ruT = dt("ruT", [DM, MINTER], F32, kind="ExternalInput").ap()
    rdT = dt("rdT", [MINTER, DM], F32, kind="ExternalInput").ap()
    ident_in = dt("ident_in", [P, P], F32, kind="ExternalInput").ap()
    maskA_in = dt("maskA_in", [P, 7, P], F32, kind="ExternalInput").ap()
    maskQ_in = dt("maskQ_in", [P, 8, P], F32, kind="ExternalInput").ap()
    hmask_in = dt("hmask_in", [P, 1], F32, kind="ExternalInput").ap()
    sel_in = dt("sel_in", [P, NCORE], F32, kind="ExternalInput").ap()
    out_d = dt("out_d", [TLOC, DM], F32, kind="ExternalOutput").ap()
    modv_d = dt("modv_d", [1, 6 * DM], F32).ap()
    silc_d = dt("silc_d", [1, DM], F32).ap()
    h1T_d = dt("h1T_d", [DM, TLOC + P], F32).ap()
    qcm_d = dt("qcm_d", [DM, TLOC], F32).ap()
    kcm_d = dt("kcm_d", [DM, TLOC], F32).ap()
    vcm_d = dt("vcm_d", [DM, TLOC], F32).ap()
    gcm_d = dt("gcm_d", [DM, TLOC], F32).ap()
    beta_d = dt("beta_d", [TLOC, NH], F32).ap()
    gate_d = dt("gate_d", [TLOC, DM], F32).ap()
    oloc_d = dt("oloc_d", [TLOC, DM], F32).ap()
    qcT_d = dt("qcT_d", [NT * NH, P, P], F32).ap()
    ogT_d = dt("ogT_d", [DM, TLOC], F32).ap()
    x2_d = dt("x2_d", [TLOC, DM], F32).ap()
    h2T_d = dt("h2T_d", [DM, TLOC], F32).ap()
    actT_d = dt("actT_d", [INTER, TLOC], F32).ap()
    actTr_d = dt("actTr_d", [MINTER, TLOC], F32).ap()
    cc_in = dt("cc_in", [3 * NH * P, P], F32).ap()
    cc_out = dt("cc_out", [NCORE, 3 * NH * P, P], F32).ap()

    es = ExitStack()
    with es:
        sems = {e: es.enter_context(nc.semaphore("sm_" + e)) for e in ENGS}
        block = es.enter_context(nc.Block())
        em = EM(block, sems)
        T = lambda fn: em.run("tensor", fn)
        V = lambda fn: em.run("vector", fn)
        A = lambda fn: em.run("scalar", fn)
        LD = lambda fn: em.run("sync", fn)
        ST = lambda fn: em.run("gpsimd", fn)
        sb = lambda name, shape: nc.sbuf_tensor(name, shape, F32)
        psb = lambda name, shape: nc.psum_tensor(name, shape, F32)

        def build_rep(dst, idx, addc, mrow, psr):
            LD(lambda px: px.dma_start(out=mrow[:], in_=modv_d[:, idx * DM:(idx + 1) * DM]))
            for ch in range(4):
                T(lambda px, ch=ch: px.matmul(psr[:], ones_r[:],
                                              mrow[:, ch * 512:(ch + 1) * 512],
                                              start=True, stop=True))
                V(lambda px, ch=ch: px.tensor_scalar(
                    out=dst[:, ch * 512:(ch + 1) * 512], in0=psr[:],
                    scalar1=addc, scalar2=None, op0=OP.add))

        with scope(sb("ident", [P, P]), sb("maskA", [P, 7, P]), sb("maskQ", [P, 8, P]), sb("hmask", [P, 1]), sb("selr", [P, NCORE]), sb("cwb", [P, 3 * NH, 4]), sb("dbb", [P, NH]), sb("negA", [P, NH]), sb("rmsr", [P, D]), sb("ones_r", [1, P]), sb("ca1T", [P, NH]), sb("saT", [P, NH]), sb("Smy", [P, NH, D])) as (ident, maskA, maskQ, hmask, selr, cwb, dbb, negA, rmsr, ones_r, ca1T, saT, Smy):

            LD(lambda px: [px.dma_start(out=ident[:], in_=ident_in),
                           px.dma_start(out=maskA[:], in_=maskA_in),
                           px.dma_start(out=maskQ[:], in_=maskQ_in),
                           px.dma_start(out=hmask[:], in_=hmask_in),
                           px.dma_start(out=selr[:], in_=sel_in),
                           px.dma_start(out=cwb[:], in_=cw_all),
                           px.dma_start(out=dbb[:], in_=db_t),
                           px.dma_start(out=negA[:], in_=Arep),
                           px.dma_start(out=rmsr[:], in_=rms_in)])
            V(lambda px: px.memset(ones_r[:], 1.0))
            A(lambda px: px.activation(out=negA[:], in_=negA[:], func=AF.Exp))
            V(lambda px: px.tensor_scalar(out=negA[:], in0=negA[:], scalar1=-1.0,
                                          scalar2=None, op0=OP.mult))

            with scope(sb("cv", [1, DM]), sb("cvT", [P, NH]), sb("wmt", [P, 512]), sb("mrow", [1, DM]), sb("srow", [1, DM]), psb("psm", [1, 512])) as (cv, cvT, wmt, mrow, srow, psm):
                LD(lambda px: px.dma_start(out=cv[:], in_=cvec))
                A(lambda px: px.activation(out=cv[:], in_=cv[:], func=AF.Silu))
                ST(lambda px: px.dma_start(out=silc_d, in_=cv[:]))
                LD(lambda px: px.dma_start(
                    out=cvT[:], in_=silc_d.rearrange("o (a p) -> p (o a)", p=P),
                    allow_slow_non_contiguous=True))
                for ch in range(24):
                    for dt_i in range(16):
                        LD(lambda px, ch=ch, dt_i=dt_i: px.dma_start(
                            out=wmt[:], in_=wmod[dt_i * P:(dt_i + 1) * P,
                                                 ch * 512:(ch + 1) * 512]))
                        T(lambda px, dt_i=dt_i: px.matmul(
                            psm[:], cvT[:, dt_i:dt_i + 1], wmt[:],
                            start=(dt_i == 0), stop=(dt_i == 15)))
                    V(lambda px: px.tensor_copy(mrow[:, 0:512], psm[:]))
                    ST(lambda px, ch=ch: px.dma_start(
                        out=modv_d[:, ch * 512:(ch + 1) * 512], in_=mrow[:, 0:512]))
                for idx in (2, 5):
                    LD(lambda px, idx=idx: px.dma_start(
                        out=mrow[:], in_=modv_d[:, idx * DM:(idx + 1) * DM]))
                    V(lambda px: px.tensor_scalar(out=srow[:], in0=mrow[:], scalar1=0.0,
                                                  scalar2=None, op0=OP.is_ge))
                    V(lambda px: px.tensor_scalar(out=srow[:], in0=srow[:], scalar1=2.0,
                                                  scalar2=-1.0, op0=OP.mult, op1=OP.add))
                    A(lambda px: px.activation(out=mrow[:], in_=mrow[:], func=AF.Abs))
                    V(lambda px: px.tensor_scalar(out=mrow[:], in0=mrow[:], scalar1=0.25,
                                                  scalar2=None, op0=OP.max))
                    V(lambda px: px.tensor_tensor(out=mrow[:], in0=mrow[:], in1=srow[:],
                                                  op=OP.mult))
                    ST(lambda px, idx=idx: px.dma_start(
                        out=modv_d[:, idx * DM:(idx + 1) * DM], in_=mrow[:]))
            LD(lambda px: px.dma_start(
                out=saT[:], in_=modv_d[:, 0:DM].rearrange("o (a p) -> p (o a)", p=P),
                allow_slow_non_contiguous=True))
            LD(lambda px: px.dma_start(
                out=ca1T[:], in_=modv_d[:, DM:2 * DM].rearrange("o (a p) -> p (o a)", p=P),
                allow_slow_non_contiguous=True))
            V(lambda px: px.tensor_scalar(out=ca1T[:], in0=ca1T[:], scalar1=1.0,
                                          scalar2=None, op0=OP.add))

            with scope(sb("xt", [P, DM]), sb("mu", [P, 1]), sb("sg2", [P, 1]), sb("xsq", [P, DM]), sb("h1t", [P, NH, P]), psb("pstr", [P, P])) as (xt, mu, sg2, xsq, h1t, pstr):
                for tt in range(NT + 1):
                    LD(lambda px, tt=tt: px.dma_start(out=xt[:], in_=x_in[tt * P:(tt + 1) * P, :]))
                    V(lambda px: px.tensor_reduce(mu[:], xt[:], AX.X, OP.add))
                    V(lambda px: px.tensor_scalar(out=mu[:], in0=mu[:], scalar1=1.0 / DM,
                                                  scalar2=None, op0=OP.mult))
                    V(lambda px: px.tensor_scalar(out=xt[:], in0=xt[:], scalar1=mu[:],
                                                  scalar2=None, op0=OP.subtract))
                    V(lambda px: px.tensor_tensor(out=xsq[:], in0=xt[:], in1=xt[:], op=OP.mult))
                    V(lambda px: px.tensor_reduce(sg2[:], xsq[:], AX.X, OP.add))
                    V(lambda px: px.tensor_scalar(out=sg2[:], in0=sg2[:], scalar1=1.0 / DM,

                                                  scalar2=1e-5, op0=OP.mult, op1=OP.add))

                    A(lambda px: px.activation(out=sg2[:], in_=sg2[:], func=AF.Sqrt))

                    V(lambda px: px.reciprocal(sg2[:], sg2[:]))
                    V(lambda px: px.tensor_scalar(out=xt[:], in0=xt[:], scalar1=sg2[:],
                                                  scalar2=None, op0=OP.mult))
                    for et in range(16):
                        T(lambda px, et=et: px.transpose(pstr[:], xt[:, et * P:(et + 1) * P],
                                                         ident[:]))
                        V(lambda px, et=et: px.tensor_scalar(
                            out=h1t[:, et, :], in0=pstr[:], scalar1=ca1T[:, et:et + 1],
                            scalar2=saT[:, et:et + 1], op0=OP.mult, op1=OP.add))
                    ST(lambda px, tt=tt: px.dma_start(
                        out=h1T_d.rearrange("(dt p) t -> p dt t", p=P)[:, :, tt * P:(tt + 1) * P],
                        in_=h1t[:]))

            with scope(sb("Wbuf", [P, 16, 1024]), sb("h1tb", [P, 16, P]), sb("qtok", [P, 1024]), sb("ssq", [P, 4]), sb("sqb", [P, P]), sb("ext", [P, 8, P + 3]), sb("tails", [P, 8, 3]), sb("cacc", [P, 8, P]), sb("cmul", [P, 8, P]), psb("psq", [P, 512]), psb("pst1", [P, P])) as (Wbuf, h1tb, qtok, ssq, sqb, ext, tails, cacc, cmul, psq, pst1):
                for ti, (Wsrc, dst_d, cwoff, has_l2) in enumerate(
                        ((Wq, qcm_d, 0, True), (Wk, kcm_d, NH, True), (Wv, vcm_d, 2 * NH, False))):
                    for hf in range(2):
                        LD(lambda px, Wsrc=Wsrc, hf=hf: px.dma_start(
                            out=Wbuf[:], in_=Wsrc.rearrange("(dt p) e -> p dt e", p=P)
                            [:, :, hf * 1024:(hf + 1) * 1024]))
                        V(lambda px: px.memset(tails[:], 0.0))
                        for tt in range(NT + 1):
                            LD(lambda px, tt=tt: px.dma_start(
                                out=h1tb[:], in_=h1T_d.rearrange("(dt p) t -> p dt t", p=P)
                                [:, :, tt * P:(tt + 1) * P]))
                            for ec in range(2):
                                for dt_i in range(16):
                                    T(lambda px, ec=ec, dt_i=dt_i: px.matmul(
                                        psq[:], h1tb[:, dt_i, :],
                                        Wbuf[:, dt_i, ec * 512:(ec + 1) * 512],
                                        start=(dt_i == 0), stop=(dt_i == 15)))
                                if has_l2:
                                    for hh in range(4):
                                        V(lambda px, ec=ec, hh=hh: px.tensor_copy(
                                            qtok[:, ec * 512 + hh * P: ec * 512 + (hh + 1) * P],
                                            psq[:, hh * P:(hh + 1) * P]))
                                        V(lambda px, ec=ec, hh=hh: px.tensor_tensor(
                                            out=sqb[:],
                                            in0=qtok[:, ec * 512 + hh * P: ec * 512 + (hh + 1) * P],
                                            in1=qtok[:, ec * 512 + hh * P: ec * 512 + (hh + 1) * P],
                                            op=OP.mult))
                                        V(lambda px, hh=hh: px.tensor_reduce(
                                            ssq[:, hh:hh + 1], sqb[:], AX.X, OP.add))
                                    V(lambda px: px.tensor_scalar(out=ssq[:], in0=ssq[:], scalar1=1.0,

                                                                  scalar2=1e-6, op0=OP.mult, op1=OP.add))

                                    A(lambda px: px.activation(out=ssq[:], in_=ssq[:], func=AF.Sqrt))
                                    V(lambda px: px.reciprocal(ssq[:], ssq[:]))
                                    for hh in range(4):
                                        V(lambda px, ec=ec, hh=hh: px.tensor_scalar(
                                            out=qtok[:, ec * 512 + hh * P: ec * 512 + (hh + 1) * P],
                                            in0=qtok[:, ec * 512 + hh * P: ec * 512 + (hh + 1) * P],
                                            scalar1=ssq[:, hh:hh + 1], scalar2=None,
                                            op0=OP.mult))
                                else:
                                    V(lambda px, ec=ec: px.tensor_copy(
                                        qtok[:, ec * 512:(ec + 1) * 512], psq[:]))
                            V(lambda px: px.tensor_copy(ext[:, :, 0:3], tails[:]))
                            for et in range(8):
                                T(lambda px, et=et: px.transpose(
                                    pst1[:], qtok[:, et * P:(et + 1) * P], ident[:]))
                                V(lambda px, et=et: px.tensor_copy(ext[:, et, 3:P + 3], pst1[:]))
                            V(lambda px: px.tensor_copy(tails[:], ext[:, :, P:P + 3]))
                            if tt == 0:
                                V(lambda px: px.tensor_scalar(
                                    out=tails[:], in0=tails[:], scalar1=hmask[:],
                                    scalar2=None, op0=OP.mult))
                                continue
                            for tau in range(4):
                                dst = cacc if tau == 0 else cmul
                                V(lambda px, cwoff=cwoff, hf=hf, tau=tau, dst=dst:
                                  px.tensor_tensor(
                                      out=dst[:], in0=ext[:, :, tau:tau + P],
                                      in1=cwb[:, cwoff + hf * 8: cwoff + hf * 8 + 8, tau:tau + 1]
                                      .broadcast_to([P, 8, P]), op=OP.mult))
                                if tau > 0:
                                    V(lambda px: px.tensor_tensor(out=cacc[:], in0=cacc[:],
                                                                  in1=cmul[:], op=OP.add))
                            A(lambda px: px.activation(out=cacc[:], in_=cacc[:], func=AF.Silu))
                            ST(lambda px, dst_d=dst_d, hf=hf, tt=tt: px.dma_start(
                                out=dst_d.rearrange("(eh p) t -> p eh t", p=P)
                                [:, hf * 8:(hf + 1) * 8, (tt - 1) * P: tt * P], in_=cacc[:]))

            with scope(sb("h1tc", [P, 16, P]), sb("Wbb", [P, 16, NH]), sb("Wfab", [P, 16, D]), sb("Wfbb", [P, DM]), sb("Wgab", [P, 16, D]), sb("Wgbb", [P, DM]), sb("f1", [P, D]), sb("f1T", [P, P]), sb("spx", [P, P]), sb("spn", [P, P]), sb("btok", [P, NH]), sb("gout", [P, NH, P]), sb("gf", [P, DM]), psb("psw", [P, DM]), psb("pss", [P, 512]), psb("pst2", [P, P])) as (h1tc, Wbb, Wfab, Wfbb, Wgab, Wgbb, f1, f1T, spx, spn, btok, gout, gf, psw, pss, pst2):
                LD(lambda px: [px.dma_start(out=Wbb[:], in_=Wb.rearrange("(dt p) h -> p dt h", p=P)),
                               px.dma_start(out=Wfab[:], in_=Wfa.rearrange("(dt p) e -> p dt e", p=P)),
                               px.dma_start(out=Wfbb[:], in_=Wfb),
                               px.dma_start(out=Wgab[:], in_=Wga.rearrange("(dt p) e -> p dt e", p=P)),
                               px.dma_start(out=Wgbb[:], in_=Wgb)])
                for tt in range(1, NT + 1):
                    LD(lambda px, tt=tt: px.dma_start(
                        out=h1tc[:], in_=h1T_d.rearrange("(dt p) t -> p dt t", p=P)
                        [:, :, tt * P:(tt + 1) * P]))
                    for dt_i in range(16):
                        T(lambda px, dt_i=dt_i: px.matmul(
                            pss[:, 0:NH], h1tc[:, dt_i, :], Wbb[:, dt_i, :],
                            start=(dt_i == 0), stop=(dt_i == 15)))
                    A(lambda px: px.activation(out=btok[:], in_=pss[:, 0:NH], func=AF.Sigmoid))
                    ST(lambda px, tt=tt: px.dma_start(
                        out=beta_d[(tt - 1) * P: tt * P, :], in_=btok[:]))
                    for dt_i in range(16):
                        T(lambda px, dt_i=dt_i: px.matmul(
                            pss[:, 0:D], h1tc[:, dt_i, :], Wfab[:, dt_i, :],
                            start=(dt_i == 0), stop=(dt_i == 15)))
                    V(lambda px: px.tensor_copy(f1[:], pss[:, 0:D]))
                    T(lambda px: px.transpose(pst2[:], f1[:], ident[:]))
                    V(lambda px: px.tensor_copy(f1T[:], pst2[:]))
                    for et in range(16):
                        T(lambda px, et=et: px.matmul(
                            psw[:, et * P:(et + 1) * P], Wfbb[:, et * P:(et + 1) * P], f1T[:],
                            start=True, stop=True))
                    for et in range(16):
                        V(lambda px, et=et: px.tensor_scalar(
                            out=spx[:], in0=psw[:, et * P:(et + 1) * P],
                            scalar1=dbb[:, et:et + 1], scalar2=None, op0=OP.add))
                        V(lambda px: px.tensor_scalar(out=spn[:], in0=spx[:], scalar1=-1.0,
                                                      scalar2=None, op0=OP.mult))
                        V(lambda px: px.tensor_tensor(out=spn[:], in0=spn[:], in1=spx[:],
                                                      op=OP.max))
                        V(lambda px: px.tensor_scalar(out=spn[:], in0=spn[:], scalar1=-1.0,
                                                      scalar2=None, op0=OP.mult))
                        A(lambda px: px.activation(out=spn[:], in_=spn[:], func=AF.Exp))
                        V(lambda px: px.tensor_scalar(out=spn[:], in0=spn[:], scalar1=1.0,
                                                      scalar2=None, op0=OP.add))
                        A(lambda px: px.activation(out=spn[:], in_=spn[:], func=AF.Ln))
                        V(lambda px: px.tensor_scalar(out=spx[:], in0=spx[:], scalar1=0.0,
                                                      scalar2=None, op0=OP.max))
                        V(lambda px, et=et: px.tensor_tensor(out=gout[:, et, :], in0=spx[:],
                                                             in1=spn[:], op=OP.add))
                        V(lambda px, et=et: px.tensor_scalar(
                            out=gout[:, et, :], in0=gout[:, et, :],
                            scalar1=negA[:, et:et + 1], scalar2=None, op0=OP.mult))
                    ST(lambda px, tt=tt: px.dma_start(
                        out=gcm_d.rearrange("(eh p) t -> p eh t", p=P)[:, :, (tt - 1) * P: tt * P],
                        in_=gout[:]))
                    for dt_i in range(16):
                        T(lambda px, dt_i=dt_i: px.matmul(
                            pss[:, 0:D], h1tc[:, dt_i, :], Wgab[:, dt_i, :],
                            start=(dt_i == 0), stop=(dt_i == 15)))
                    V(lambda px: px.tensor_copy(f1[:], pss[:, 0:D]))
                    T(lambda px: px.transpose(pst2[:], f1[:], ident[:]))
                    V(lambda px: px.tensor_copy(f1T[:], pst2[:]))
                    for ch in range(4):
                        T(lambda px, ch=ch: px.matmul(
                            psw[:, ch * 512:(ch + 1) * 512], f1T[:],
                            Wgbb[:, ch * 512:(ch + 1) * 512], start=True, stop=True))
                    V(lambda px: px.tensor_copy(gf[:], psw[:]))
                    ST(lambda px, tt=tt: px.dma_start(
                        out=gate_d[(tt - 1) * P: tt * P, :], in_=gf[:]))

            with scope(sb("kw", [P, NH, P]), sb("kbw", [P, NH, P]), sb("qsw", [P, NH, P]), sb("gw", [P, NH, P]), sb("cw_", [P, NH, P]), sb("rw", [P, NH, P]), sb("t1w", [P, NH, P]), sb("t2w", [P, NH, P]), sb("Ktw", [P, NH, P]), sb("Khw", [P, NH, P]), sb("X0w", [P, NH, P]), sb("Y0w", [P, NH, P]), sb("AqTw", [P, NH, P]), sb("Decs", [P, NH]), sb("brow", [1, NH, P]), sb("vh", [P, P]), sb("sol", [P, 256]), sb("Xa", [P, P]), sb("Ya", [P, P]), sb("Xb", [P, P]), sb("Yb", [P, P]), sb("Kbtok", [P, P]), sb("QeffT", [P, P]), sb("Lb", [P, P]), sb("LTb", [P, P]), sb("obuf", [P, P]), sb("Lcum", [P, NH, P]), sb("LcumT", [P, NH, P]), sb("Bcum", [P, NH, P]), psb("psE", [P, NH, P]), psb("psS", [P, 256]), psb("pst3", [P, P]), psb("psP", [P, P]), psb("psP2", [P, P])) as (kw, kbw, qsw, gw, cw_, rw, t1w, t2w, Ktw, Khw, X0w, Y0w, AqTw, Decs, brow, vh, sol, Xa, Ya, Xb, Yb, Kbtok, QeffT, Lb, LTb, obuf, Lcum, LcumT, Bcum, psE, psS, pst3, psP, psP2):
                V(lambda px: px.memset(Bcum[:], 0.0))
                for h in range(NH):
                    V(lambda px, h=h: px.tensor_copy(Lcum[:, h, :], ident[:]))
                    V(lambda px, h=h: px.tensor_copy(LcumT[:, h, :], ident[:]))
                for cc in range(NT):
                    tsl = slice(cc * P, (cc + 1) * P)
                    LD(lambda px, tsl=tsl: [
                        px.dma_start(out=qsw[:],
                                     in_=qcm_d.rearrange("(eh p) t -> p eh t", p=P)[:, :, tsl]),
                        px.dma_start(out=kw[:],
                                     in_=kcm_d.rearrange("(eh p) t -> p eh t", p=P)[:, :, tsl]),
                        px.dma_start(out=gw[:],
                                     in_=gcm_d.rearrange("(eh p) t -> p eh t", p=P)[:, :, tsl]),
                        px.dma_start(out=brow[:],
                                     in_=beta_d[tsl, :].rearrange("t h -> () h t"),
                                     allow_slow_non_contiguous=True)])
                    for h in range(NH):
                        T(lambda px, h=h: px.matmul(psE[:, h, :], ones_r[:],
                                                    brow[:, h, :],
                                                    start=True, stop=True))
                    V(lambda px: px.tensor_tensor(out=kbw[:], in0=kw[:], in1=psE[:], op=OP.mult))
                    V(lambda px: px.tensor_scalar(out=qsw[:], in0=qsw[:], scalar1=SCALE,
                                                  scalar2=None, op0=OP.mult))
                    V(lambda px: px.tensor_copy(cw_[:], gw[:]))
                    V(lambda px: px.tensor_copy(rw[:], gw[:]))
                    V(lambda px: px.memset(X0w[:], 0.0))
                    V(lambda px: px.memset(AqTw[:], 0.0))
                    for lv in range(7):
                        s = 1 << lv
                        V(lambda px: px.tensor_tensor(out=t1w[:], in0=rw[:], in1=gw[:],
                                                      op=OP.subtract))
                        A(lambda px: px.activation(out=t1w[:], in_=t1w[:], func=AF.Exp))
                        V(lambda px: px.tensor_tensor(out=Khw[:], in0=kbw[:], in1=t1w[:],
                                                      op=OP.mult))
                        A(lambda px: px.activation(out=t1w[:], in_=cw_[:], func=AF.Exp))
                        V(lambda px: px.tensor_tensor(out=Ktw[:], in0=kw[:], in1=t1w[:],
                                                      op=OP.mult))
                        for h in range(NH):
                            T(lambda px, h=h: px.matmul(psE[:, h, :], Ktw[:, h, :], Khw[:, h, :],
                                                        start=True, stop=True))
                        V(lambda px, lv=lv: px.tensor_tensor(
                            out=t2w[:], in0=psE[:],
                            in1=maskA[:, lv:lv + 1, :].broadcast_to([P, NH, P]), op=OP.mult))
                        V(lambda px: px.tensor_tensor(out=X0w[:], in0=X0w[:], in1=t2w[:],
                                                      op=OP.add))
                        V(lambda px: px.tensor_tensor(out=Ktw[:], in0=qsw[:], in1=t1w[:],
                                                      op=OP.mult))
                        for h in range(NH):
                            T(lambda px, h=h: px.matmul(psE[:, h, :], Khw[:, h, :], Ktw[:, h, :],
                                                        start=True, stop=True))
                        V(lambda px, lv=lv: px.tensor_tensor(
                            out=t2w[:], in0=psE[:],
                            in1=maskQ[:, lv:lv + 1, :].broadcast_to([P, NH, P]), op=OP.mult))
                        V(lambda px: px.tensor_tensor(out=AqTw[:], in0=AqTw[:], in1=t2w[:],
                                                      op=OP.add))
                        V(lambda px: px.tensor_tensor(out=t1w[:], in0=cw_[:], in1=rw[:], op=OP.add))
                        V(lambda px: px.tensor_tensor(out=t1w[:], in0=t1w[:], in1=gw[:],
                                                      op=OP.subtract))
                        cf = cw_.rearrange("p h t -> p (h t)")
                        rf = rw.rearrange("p h t -> p (h t)")
                        tf = t1w.rearrange("p h t -> p (h t)")
                        def hi(ap, s=s):
                            return ap.rearrange("p (g two s) -> p g two s", two=2, s=s)[:, :, 1, :]
                        def lo(ap, s=s):
                            return ap.rearrange("p (g two s) -> p g two s", two=2, s=s)[:, :, 0, :]
                        V(lambda px, cf=cf, tf=tf, hi=hi, lo=lo: px.tensor_tensor(
                            out=hi(cf), in0=hi(cf), in1=lo(tf), op=OP.add))
                        V(lambda px, rf=rf, tf=tf, hi=hi, lo=lo: px.tensor_tensor(
                            out=lo(rf), in0=lo(rf), in1=hi(tf), op=OP.add))
                    for h in range(NH):
                        T(lambda px, h=h: px.matmul(psE[:, h, :], kbw[:, h, :], qsw[:, h, :],
                                                    start=True, stop=True))
                    V(lambda px: px.tensor_tensor(
                        out=t2w[:], in0=psE[:],
                        in1=maskQ[:, 7:8, :].broadcast_to([P, NH, P]), op=OP.mult))
                    V(lambda px: px.tensor_tensor(out=AqTw[:], in0=AqTw[:], in1=t2w[:], op=OP.add))
                    A(lambda px: px.activation(out=t1w[:], in_=cw_[:], func=AF.Exp))
                    V(lambda px: px.tensor_copy(Decs[:], t1w[:, :, P - 1]))
                    V(lambda px: px.tensor_tensor(out=Ktw[:], in0=kw[:], in1=t1w[:], op=OP.mult))
                    V(lambda px: px.tensor_tensor(out=qsw[:], in0=qsw[:], in1=t1w[:], op=OP.mult))
                    V(lambda px: px.tensor_tensor(out=t1w[:], in0=rw[:], in1=gw[:], op=OP.subtract))
                    A(lambda px: px.activation(out=t1w[:], in_=t1w[:], func=AF.Exp))
                    V(lambda px: px.tensor_tensor(out=Khw[:], in0=kbw[:], in1=t1w[:], op=OP.mult))
                    for h in range(NH):
                        T(lambda px, h=h: px.transpose(pst3[:], X0w[:, h, :], ident[:]))
                        V(lambda px, h=h: px.tensor_copy(Y0w[:, h, :], pst3[:]))
                    for h in range(NH):
                        LD(lambda px, h=h, tsl=tsl: px.dma_start(
                            out=vh[:], in_=vcm_d[h * P:(h + 1) * P, tsl]))
                        T(lambda px: px.transpose(pst3[:], vh[:], ident[:]))
                        V(lambda px: px.tensor_copy(sol[:, 0:P], pst3[:]))
                        T(lambda px, h=h: px.transpose(pst3[:], Ktw[:, h, :], ident[:]))
                        V(lambda px: px.tensor_copy(sol[:, P:2 * P], pst3[:]))
                        V(lambda px, h=h: px.tensor_copy(Xa[:], X0w[:, h, :]))
                        V(lambda px, h=h: px.tensor_copy(Ya[:], Y0w[:, h, :]))
                        Xc, Yc, Xn, Yn = Xa, Ya, Xb, Yb
                        for i in range(7):
                            T(lambda px, Yc=Yc: px.matmul(psS[:], Yc[:], sol[:],
                                                          start=True, stop=True))
                            V(lambda px: px.tensor_tensor(out=sol[:], in0=sol[:], in1=psS[:],
                                                          op=OP.add))
                            if i < 6:
                                T(lambda px, Xc=Xc, Yc=Yc: px.matmul(pst3[:], Yc[:], Xc[:],
                                                                     start=True, stop=True))
                                V(lambda px, Xn=Xn: px.tensor_copy(Xn[:], pst3[:]))
                                T(lambda px, Xc=Xc, Yc=Yc: px.matmul(pst3[:], Xc[:], Yc[:],
                                                                     start=True, stop=True))
                                V(lambda px, Yn=Yn: px.tensor_copy(Yn[:], pst3[:]))
                                Xc, Xn = Xn, Xc
                                Yc, Yn = Yn, Yc
                        T(lambda px, h=h: px.transpose(pst3[:], Khw[:, h, :], ident[:]))
                        V(lambda px: px.tensor_copy(Kbtok[:], pst3[:]))
                        T(lambda px, h=h: px.matmul(psP[:], sol[:, P:2 * P], AqTw[:, h, :],
                                                    start=True, stop=True))
                        V(lambda px, h=h: px.tensor_tensor(out=QeffT[:], in0=qsw[:, h, :],
                                                           in1=psP[:], op=OP.subtract))
                        T(lambda px, h=h: px.matmul(psP[:], AqTw[:, h, :], sol[:, 0:P],
                                                    start=True, stop=False))
                        T(lambda px, h=h: px.matmul(psP[:], QeffT[:], Bcum[:, h, :],
                                                    start=False, stop=True))
                        V(lambda px: px.tensor_copy(obuf[:], psP[:]))
                        ST(lambda px, h=h, cc=cc: px.dma_start(
                            out=oloc_d[cc * P:(cc + 1) * P, h * P:(h + 1) * P], in_=obuf[:]))
                        T(lambda px, h=h: px.matmul(psP[:], Lcum[:, h, :], QeffT[:],
                                                    start=True, stop=True))
                        V(lambda px: px.tensor_copy(obuf[:], psP[:]))
                        ST(lambda px, h=h, cc=cc: px.dma_start(
                            out=qcT_d[cc * NH + h, :, :], in_=obuf[:]))
                        T(lambda px: px.matmul(psP[:], Kbtok[:], sol[:, P:2 * P],
                                               start=True, stop=True))
                        T(lambda px: px.matmul(psP2[:], sol[:, P:2 * P], Kbtok[:],
                                               start=True, stop=True))
                        V(lambda px, h=h: px.tensor_scalar(out=Lb[:], in0=ident[:],
                                                           scalar1=Decs[:, h:h + 1],
                                                           scalar2=None, op0=OP.mult))
                        V(lambda px: px.tensor_copy(LTb[:], Lb[:]))
                        V(lambda px: px.tensor_tensor(out=Lb[:], in0=Lb[:], in1=psP[:],
                                                      op=OP.subtract))
                        V(lambda px: px.tensor_tensor(out=LTb[:], in0=LTb[:], in1=psP2[:],
                                                      op=OP.subtract))
                        T(lambda px: px.matmul(psP[:], Kbtok[:], sol[:, 0:P],
                                               start=True, stop=False))
                        T(lambda px, h=h: px.matmul(psP[:], LTb[:], Bcum[:, h, :],
                                                    start=False, stop=True))
                        T(lambda px, h=h: px.matmul(psP2[:], LTb[:], Lcum[:, h, :],
                                                    start=True, stop=True))
                        T(lambda px, h=h: px.matmul(pst3[:], Lcum[:, h, :], LTb[:],
                                                    start=True, stop=True))
                        V(lambda px, h=h: px.tensor_copy(Bcum[:, h, :], psP[:]))
                        V(lambda px, h=h: px.tensor_copy(Lcum[:, h, :], psP2[:]))
                        V(lambda px, h=h: px.tensor_copy(LcumT[:, h, :], pst3[:]))
                ST(lambda px: [
                    px.dma_start(out=cc_in[0:NH * P, :]
                                 .rearrange("(h p) k -> p h k", p=P), in_=Lcum[:]),
                    px.dma_start(out=cc_in[NH * P:2 * NH * P, :]
                                 .rearrange("(h p) k -> p h k", p=P), in_=LcumT[:]),
                    px.dma_start(out=cc_in[2 * NH * P:3 * NH * P, :]
                                 .rearrange("(h p) k -> p h k", p=P), in_=Bcum[:])])
                em.wait_all("gpsimd")

                def coll(px):
                    ins = px.collective_compute(
                        "AllGather", OP.bypass, replica_groups=[list(range(NCORE))],
                        ins=[cc_in], outs=[cc_out])
                    ins.then_inc(sems["gpsimd"], 1)
                    em.cnt["gpsimd"] += 1
                    return []
                em.run("gpsimd", coll, self_wait=True)
                V(lambda px: px.memset(Smy[:], 0.0))
                V(lambda px: px.memset(t1w[:], 0.0))
                for j in range(NCORE):
                    V(lambda px, j=j: px.tensor_scalar(out=t2w[:], in0=t1w[:],
                                                       scalar1=selr[:, j:j + 1],
                                                       scalar2=None, op0=OP.mult))
                    V(lambda px: px.tensor_tensor(out=Smy[:], in0=Smy[:], in1=t2w[:], op=OP.add))
                    if j < NCORE - 1:
                        LD(lambda px, j=j: [
                            px.dma_start(out=LcumT[:], in_=cc_out[j, NH * P:2 * NH * P, :]
                                         .rearrange("(h p) k -> p h k", p=P)),
                            px.dma_start(out=Bcum[:], in_=cc_out[j, 2 * NH * P:3 * NH * P, :]
                                         .rearrange("(h p) k -> p h k", p=P))])
                        for h in range(NH):
                            T(lambda px, h=h: px.matmul(psP[:], LcumT[:, h, :], t1w[:, h, :],
                                                        start=True, stop=False))
                            T(lambda px, h=h: px.matmul(psP[:], ident[:], Bcum[:, h, :],
                                                        start=False, stop=True))
                            V(lambda px, h=h: px.tensor_copy(t1w[:, h, :], psP[:]))

            with scope(sb("qct", [P, P]), sb("ob2", [P, P]), sb("osq", [P, P]), sb("ssum", [P, 1]), sb("gt", [P, P]), sb("orow", [P, NH, P]), sb("orT", [P, NH, P]), psb("psO", [P, P]), psb("pst4", [P, P])) as (qct, ob2, osq, ssum, gt, orow, orT, psO, pst4):
                for cc in range(NT):
                    for h in range(NH):
                        LD(lambda px, cc=cc, h=h: [
                            px.dma_start(out=qct[:], in_=qcT_d[cc * NH + h, :, :]),
                            px.dma_start(out=ob2[:], in_=oloc_d[cc * P:(cc + 1) * P,
                                                               h * P:(h + 1) * P]),
                            px.dma_start(out=gt[:], in_=gate_d[cc * P:(cc + 1) * P,
                                                               h * P:(h + 1) * P])])
                        T(lambda px, h=h: px.matmul(psO[:], qct[:], Smy[:, h, :],
                                                    start=True, stop=True))
                        V(lambda px: px.tensor_tensor(out=ob2[:], in0=ob2[:], in1=psO[:],
                                                      op=OP.add))
                        V(lambda px: px.tensor_tensor(out=osq[:], in0=ob2[:], in1=ob2[:],
                                                      op=OP.mult))
                        V(lambda px: px.tensor_reduce(ssum[:], osq[:], AX.X, OP.add))
                        V(lambda px: px.tensor_scalar(out=ssum[:], in0=ssum[:], scalar1=1.0 / D,

                                                      scalar2=1e-5, op0=OP.mult, op1=OP.add))

                        A(lambda px: px.activation(out=ssum[:], in_=ssum[:], func=AF.Sqrt))

                        V(lambda px: px.reciprocal(ssum[:], ssum[:]))
                        V(lambda px: px.tensor_scalar(out=ob2[:], in0=ob2[:], scalar1=ssum[:],
                                                      scalar2=None, op0=OP.mult))
                        V(lambda px: px.tensor_tensor(out=ob2[:], in0=ob2[:], in1=rmsr[:],
                                                      op=OP.mult))
                        A(lambda px: px.activation(out=gt[:], in_=gt[:], func=AF.Sigmoid))
                        V(lambda px, h=h: px.tensor_tensor(out=orow[:, h, :], in0=ob2[:],
                                                           in1=gt[:], op=OP.mult))
                    for et in range(NH):
                        T(lambda px, et=et: px.transpose(pst4[:], orow[:, et, :], ident[:]))
                        V(lambda px, et=et: px.tensor_copy(orT[:, et, :], pst4[:]))
                    ST(lambda px, cc=cc: px.dma_start(
                        out=ogT_d.rearrange("(eh p) t -> p eh t", p=P)[:, :, cc * P:(cc + 1) * P],
                        in_=orT[:]))

            with scope(sb("Wob", [P, 16, 1024]), sb("ogt", [P, 16, P]), sb("xres", [P, 1024]), sb("x2b", [P, 1024]), sb("garep", [P, DM]), sb("mrow2", [1, DM]), psb("psW", [P, 1024]), psb("psr2", [P, 512])) as (Wob, ogt, xres, x2b, garep, mrow2, psW, psr2):
                build_rep(garep, 2, 0.0, mrow2, psr2)
                for hf in range(2):
                    LD(lambda px, hf=hf: px.dma_start(
                        out=Wob[:], in_=Wo.rearrange("(et p) d2 -> p et d2", p=P)
                        [:, :, hf * 1024:(hf + 1) * 1024]))
                    for tt in range(NT):
                        LD(lambda px, tt=tt, hf=hf: [
                            px.dma_start(out=ogt[:],
                                         in_=ogT_d.rearrange("(eh p) t -> p eh t", p=P)
                                         [:, :, tt * P:(tt + 1) * P]),
                            px.dma_start(out=xres[:],
                                         in_=x_in[P + tt * P: P + (tt + 1) * P,
                                                  hf * 1024:(hf + 1) * 1024])])
                        for ch in range(2):
                            for et in range(16):
                                T(lambda px, ch=ch, et=et: px.matmul(
                                    psW[:, ch * 512:(ch + 1) * 512], ogt[:, et, :],
                                    Wob[:, et, ch * 512:(ch + 1) * 512],
                                    start=(et == 0), stop=(et == 15)))
                        V(lambda px, hf=hf: px.tensor_tensor(
                            out=x2b[:], in0=psW[:], in1=garep[:, hf * 1024:(hf + 1) * 1024],
                            op=OP.mult))
                        V(lambda px: px.tensor_tensor(out=x2b[:], in0=x2b[:], in1=xres[:],
                                                      op=OP.add))
                        ST(lambda px, tt=tt, hf=hf: px.dma_start(
                            out=x2_d[tt * P:(tt + 1) * P, hf * 1024:(hf + 1) * 1024],
                            in_=x2b[:]))

            with scope(sb("xt2", [P, DM]), sb("mu2", [P, 1]), sb("v2", [P, 1]), sb("xq2", [P, DM]), sb("h2t", [P, NH, P]), sb("cb1rep", [P, DM]), sb("sbrep", [P, DM]), sb("mrow3", [1, DM]), psb("pst5", [P, P]), psb("psr3", [P, 512])) as (xt2, mu2, v2, xq2, h2t, cb1rep, sbrep, mrow3, pst5, psr3):
                build_rep(cb1rep, 4, 1.0, mrow3, psr3)
                build_rep(sbrep, 3, 0.0, mrow3, psr3)
                for tt in range(NT):
                    LD(lambda px, tt=tt: px.dma_start(out=xt2[:],
                                                      in_=x2_d[tt * P:(tt + 1) * P, :]))
                    V(lambda px: px.tensor_reduce(mu2[:], xt2[:], AX.X, OP.add))
                    V(lambda px: px.tensor_scalar(out=mu2[:], in0=mu2[:], scalar1=1.0 / DM,
                                                  scalar2=None, op0=OP.mult))
                    V(lambda px: px.tensor_scalar(out=xt2[:], in0=xt2[:], scalar1=mu2[:],
                                                  scalar2=None, op0=OP.subtract))
                    V(lambda px: px.tensor_tensor(out=xq2[:], in0=xt2[:], in1=xt2[:], op=OP.mult))
                    V(lambda px: px.tensor_reduce(v2[:], xq2[:], AX.X, OP.add))
                    V(lambda px: px.tensor_scalar(out=v2[:], in0=v2[:], scalar1=1.0 / DM,

                                                  scalar2=1e-5, op0=OP.mult, op1=OP.add))

                    A(lambda px: px.activation(out=v2[:], in_=v2[:], func=AF.Sqrt))

                    V(lambda px: px.reciprocal(v2[:], v2[:]))
                    V(lambda px: px.tensor_scalar(out=xt2[:], in0=xt2[:], scalar1=v2[:],
                                                  scalar2=None, op0=OP.mult))
                    V(lambda px: px.tensor_tensor(out=xt2[:], in0=xt2[:], in1=cb1rep[:],
                                                  op=OP.mult))
                    V(lambda px: px.tensor_tensor(out=xt2[:], in0=xt2[:], in1=sbrep[:],
                                                  op=OP.add))
                    for et in range(16):
                        T(lambda px, et=et: px.transpose(pst5[:], xt2[:, et * P:(et + 1) * P],
                                                         ident[:]))
                        V(lambda px, et=et: px.tensor_copy(h2t[:, et, :], pst5[:]))
                    ST(lambda px, tt=tt: px.dma_start(
                        out=h2T_d.rearrange("(dt p) t -> p dt t", p=P)[:, :, tt * P:(tt + 1) * P],
                        in_=h2t[:]))

            with scope(sb("h2half", [P, 16, 1024]), sb("Wg4", [P, 16, 512]), sb("Wu4", [P, 16, 512]), sb("sact", [P, 4, 512]), sb("aout", [P, 4, 512]), psb("psA4", [P, 4, 512]), psb("psB4", [P, 4, 512])) as (h2half, Wg4, Wu4, sact, aout, psA4, psB4):
                for g2 in range(2):
                    LD(lambda px, g2=g2: px.dma_start(
                        out=h2half[:], in_=h2T_d.rearrange("(dt p) t -> p dt t", p=P)
                        [:, :, g2 * 1024:(g2 + 1) * 1024]))
                    for (gsrc, usrc, adst, nib) in ((shgT, shuT, actT_d, 16),
                                                    (rgT, ruT, actTr_d, 2)):
                        for ib in range(nib):
                            LD(lambda px, gsrc=gsrc, usrc=usrc, ib=ib: [
                                px.dma_start(out=Wg4[:], in_=gsrc.rearrange(
                                    "(dt p) i -> p dt i", p=P)[:, :, ib * 512:(ib + 1) * 512]),
                                px.dma_start(out=Wu4[:], in_=usrc.rearrange(
                                    "(dt p) i -> p dt i", p=P)[:, :, ib * 512:(ib + 1) * 512])])
                            for tc in range(2):
                                for it in range(4):
                                    for dt_i in range(16):
                                        T(lambda px, it=it, dt_i=dt_i, tc=tc: px.matmul(
                                            psA4[:, it, :], Wg4[:, dt_i, it * P:(it + 1) * P],
                                            h2half[:, dt_i, tc * 512:(tc + 1) * 512],
                                            start=(dt_i == 0), stop=(dt_i == 15)))
                                for it in range(4):
                                    for dt_i in range(16):
                                        T(lambda px, it=it, dt_i=dt_i, tc=tc: px.matmul(
                                            psB4[:, it, :], Wu4[:, dt_i, it * P:(it + 1) * P],
                                            h2half[:, dt_i, tc * 512:(tc + 1) * 512],
                                            start=(dt_i == 0), stop=(dt_i == 15)))
                                A(lambda px: px.activation(out=sact[:], in_=psA4[:],
                                                           func=AF.Silu))
                                V(lambda px: px.tensor_tensor(out=aout[:], in0=sact[:],
                                                              in1=psB4[:], op=OP.mult))
                                ST(lambda px, adst=adst, ib=ib, g2=g2, tc=tc: px.dma_start(
                                    out=adst.rearrange("(ib it p) t -> p ib it t", p=P, it=4)
                                    [:, ib, :, g2 * 1024 + tc * 512: g2 * 1024 + (tc + 1) * 512],
                                    in_=aout[:]))

            with scope(sb("Wd4", [P, 4, 1024]), sb("atile", [P, 4, P]), sb("ybuf", [P, NT, 1024]), sb("x2r", [P, 1024]), sb("gbrep", [P, DM]), sb("mrow4", [1, DM]), psb("psY", [P, 1024]), psb("psr4", [P, 512])) as (Wd4, atile, ybuf, x2r, gbrep, mrow4, psY, psr4):
                build_rep(gbrep, 5, 0.0, mrow4, psr4)
                for oh in range(2):
                    osl = slice(oh * 1024, (oh + 1) * 1024)
                    for ibg, (asrc, dsrc, nib) in enumerate(((actT_d, shdT, 16),
                                                             (actTr_d, rdT, 2))):
                        for ib in range(nib):
                            LD(lambda px, dsrc=dsrc, ib=ib, osl=osl: px.dma_start(
                                out=Wd4[:], in_=dsrc.rearrange(
                                    "(ib it p) d -> p ib it d", p=P, it=4)[:, ib, :, osl]))
                            for tt in range(NT):
                                LD(lambda px, asrc=asrc, ib=ib, tt=tt: px.dma_start(
                                    out=atile[:], in_=asrc.rearrange(
                                        "(ib it p) t -> p ib it t", p=P, it=4)
                                    [:, ib, :, tt * P:(tt + 1) * P]))
                                for oc in range(2):
                                    for it in range(4):
                                        T(lambda px, oc=oc, it=it: px.matmul(
                                            psY[:, oc * 512:(oc + 1) * 512], atile[:, it, :],
                                            Wd4[:, it, oc * 512:(oc + 1) * 512],
                                            start=(it == 0), stop=(it == 3)))
                                if ibg == 0 and ib == 0:
                                    V(lambda px, tt=tt: px.tensor_copy(ybuf[:, tt, :], psY[:]))
                                else:
                                    V(lambda px, tt=tt: px.tensor_tensor(
                                        out=ybuf[:, tt, :], in0=ybuf[:, tt, :], in1=psY[:],
                                        op=OP.add))
                    for tt in range(NT):
                        LD(lambda px, tt=tt, osl=osl: px.dma_start(
                            out=x2r[:], in_=x2_d[tt * P:(tt + 1) * P, osl]))
                        V(lambda px, tt=tt, oh=oh: px.tensor_tensor(
                            out=ybuf[:, tt, :], in0=ybuf[:, tt, :],
                            in1=gbrep[:, oh * 1024:(oh + 1) * 1024], op=OP.mult))
                        V(lambda px, tt=tt: px.tensor_tensor(
                            out=ybuf[:, tt, :], in0=ybuf[:, tt, :], in1=x2r[:], op=OP.add))
                        ST(lambda px, tt=tt, osl=osl: px.dma_start(
                            out=out_d[tt * P:(tt + 1) * P, osl], in_=ybuf[:, tt, :]))
            em.wait_all("gpsimd")
    return nc


def kernel(**inputs):
    x = np.asarray(inputs["L0_attn_hidden_states"], np.float32).reshape(16384, DM)
    ident = np.eye(P, dtype=np.float32)
    maskA = np.zeros((P, 7, P), np.float32)
    maskQ = np.zeros((P, 8, P), np.float32)
    for lv in range(7):
        s = 1 << lv
        j = np.arange(P)[:, None]
        l = np.arange(P)[None, :]
        owned = (j // (2 * s) == l // (2 * s)) & (j % (2 * s) >= s) & (l % (2 * s) < s)
        maskA[:, lv, :] = np.where(owned, -1.0, 0.0)
        maskQ[:, lv, :] = np.where(owned.T, 1.0, 0.0)
    maskQ[:, 7, :] = ident
    cw = np.zeros((P, 3 * NH, 4), np.float32)
    for k, nm in enumerate(("L0_attn_conv_q", "L0_attn_conv_k", "L0_attn_conv_v")):
        w = np.asarray(inputs[nm], np.float32).reshape(4, DM)
        cw[:, k * NH:(k + 1) * NH, :] = w.T.reshape(NH, P, 4).transpose(1, 0, 2)
    db = np.ascontiguousarray(np.asarray(inputs["L0_attn_dt_bias"], np.float32)
                              .reshape(NH, P).T)
    Arep = np.ascontiguousarray(np.broadcast_to(
        np.asarray(inputs["L0_attn_A_log"], np.float32).reshape(1, NH), (P, NH)))
    rmsr = np.ascontiguousarray(np.broadcast_to(
        np.asarray(inputs["L0_attn_rms_scale"], np.float32).reshape(1, D), (P, D)))
    common = {
        "Wq": np.ascontiguousarray(inputs["L0_attn_Wq"], np.float32),
        "Wk": np.ascontiguousarray(inputs["L0_attn_Wk"], np.float32),
        "Wv": np.ascontiguousarray(inputs["L0_attn_Wv"], np.float32),
        "Wb": np.ascontiguousarray(inputs["L0_attn_Wb"], np.float32),
        "Wfa": np.ascontiguousarray(inputs["L0_attn_Wfa"], np.float32),
        "Wfb": np.ascontiguousarray(inputs["L0_attn_Wfb"], np.float32),
        "Wga": np.ascontiguousarray(inputs["L0_attn_Wga"], np.float32),
        "Wgb": np.ascontiguousarray(inputs["L0_attn_Wgb"], np.float32),
        "Wo": np.ascontiguousarray(inputs["L0_attn_Wo"], np.float32),
        "cw_all": cw, "db_t": db, "Arep": Arep, "rms_in": rmsr,
        "cvec": np.ascontiguousarray(np.asarray(inputs["L0_c"], np.float32).reshape(1, DM)),
        "wmod": np.ascontiguousarray(inputs["L0_wmod"], np.float32),
        "shgT": np.ascontiguousarray(np.asarray(inputs["L0_ffn_sh_gate"], np.float32).T),
        "shuT": np.ascontiguousarray(np.asarray(inputs["L0_ffn_sh_up"], np.float32).T),
        "shdT": np.ascontiguousarray(np.asarray(inputs["L0_ffn_sh_down"], np.float32).T),
        "rgT": np.ascontiguousarray(np.asarray(inputs["L0_ffn_r_gate"], np.float32).T),
        "ruT": np.ascontiguousarray(np.asarray(inputs["L0_ffn_r_up"], np.float32).T),
        "rdT": np.ascontiguousarray(np.asarray(inputs["L0_ffn_r_down"], np.float32).T),
        "ident_in": ident, "maskA_in": maskA, "maskQ_in": maskQ,
    }
    in_maps = []
    for i in range(NCORE):
        halo = np.zeros((P, DM), np.float32) if i == 0 else x[i * TLOC - P: i * TLOC]
        xsh = np.concatenate([halo, x[i * TLOC:(i + 1) * TLOC]], axis=0)
        m = dict(common)
        m["x_in"] = np.ascontiguousarray(xsh)
        m["hmask_in"] = np.full((P, 1), 0.0 if i == 0 else 1.0, np.float32)
        sel = np.zeros((P, NCORE), np.float32)
        sel[:, i] = 1.0
        m["sel_in"] = sel
        in_maps.append(m)
    nc = build()
    res = run_bass_kernel_spmd(nc, in_maps, core_ids=list(range(NCORE)))
    out = np.concatenate([res.results[i]["out_d"] for i in range(NCORE)], axis=0)
    return out.reshape(1, 16384, DM)
